# version 01
# baseline (speedup 1.0000x reference)
import numpy as np
import concourse.bass as bass
import concourse.mybir as mybir
from concourse.bass_utils import run_bass_kernel_spmd
from concourse.aot_env import allow_ici_all_to_all

F32 = mybir.dt.float32
BF16 = mybir.dt.bfloat16
I32 = mybir.dt.int32
U8 = mybir.dt.uint8
AF = mybir.ActivationFunctionType
ALU = mybir.AluOpType
AXX = mybir.AxisListType.X

NC8 = 8
T = 8192
D = 2048
DI = 4096
DFF = 8192
TL = 1024
HALO = 128
TH = TL + HALO
EPS = 1e-5


class P:

    def __init__(self, nc, stack):
        self.nc = nc
        self.stack = stack
        self.ops = []
        self.phase = 0
        self.cnt = 0
        self.sems = [stack.enter_context(nc.semaphore("ph0"))]

    def new_phase(self):
        self.phase += 1
        self.cnt = 0
        self.sems.append(self.stack.enter_context(self.nc.semaphore(f"ph{self.phase}")))

    def add(self, eng, fn, dma=False):
        self.ops.append((eng, fn, 16 if dma else 1, self.phase, self.cnt))
        self.cnt += 16 if dma else 1

    def dma(self, eng, out, in_):
        self.add(eng, lambda e: e.dma_start(out=out, in_=in_), dma=True)

    def dmaT(self, eng, out, in_):
        self.add(eng, lambda e: e.dma_start_transpose(out, in_), dma=True)

    def emit(self, block):
        seq = self.ops
        sems = self.sems
        fph, fcnt = self.phase, self.cnt

        def mk(name):
            def body(e):
                for i, (eng, fn, inc, ph, before) in enumerate(seq):
                    if eng != name:
                        continue
                    if i > 0:
                        _, _, pinc, pph, pbefore = seq[i - 1]
                        e.wait_ge(sems[pph], pbefore + pinc)
                    fn(e).then_inc(sems[ph], inc)
                if name == "gpsimd":
                    e.wait_ge(sems[fph], fcnt)
            return body

        block.sync(mk("sync"))
        block.scalar(mk("scalar"))
        block.vector(mk("vector"))
        block.tensor(mk("tensor"))
        block.gpsimd(mk("gpsimd"))


def build_program():
    import contextlib
    nc = bass.Bass("TRN2", target_bir_lowering=False, debug=False)

    def inp(name, shape, dt=F32):
        return nc.dram_tensor(name, list(shape), dt, kind="ExternalInput").ap()

    xs = inp("xs", [TH, D])
    win_t = inp("win_t", [64, 128, 2048])
    cw = inp("cw", [32, 128, 4])
    cb = inp("cb", [32, 128, 1])
    wx_t = inp("wx_t", [32, 128, 160])
    wdt = inp("wdt", [4, 128, 128])
    dtb = inp("dtb", [4, 128, 1])
    Ain = inp("Ain", [4, 128, 16])
    Din = inp("Din", [32, 128, 1])
    wout_t = inp("wout_t", [32, 128, 2048])
    w1_t = inp("w1_t", [64, 128, 2048])
    w2_t = inp("w2_t", [64, 128, 2048])
    cvec = inp("cvec", [1, D])
    wmod_j = inp("wmod_j", [16, 128, 1536])
    mask3 = inp("mask3", [128, 3])

    out = nc.dram_tensor("out", [TL, D], F32, kind="ExternalOutput").ap()

    drh = {}

    def idram(name, shape, dt=F32):
        drh[name] = nc.dram_tensor(name, list(shape), dt)
        return drh[name].ap()

    agm_in = idram("agm_in", [1, 1536])
    agm_out = idram("agm_out", [8, 1536])
    a2a_in = idram("a2a_in", [8, 4, 128, TL], BF16)
    a2a_out = idram("a2a_out", [8, 4, 128, TL], BF16)
    ag_in = idram("ag_in", [160, TL])
    ag_out = idram("ag_out", [8, 160, TL])
    a2y_in = idram("a2y_in", [8, 4, 128, TL], BF16)
    a2y_out = idram("a2y_out", [8, 4, 128, TL], BF16)
    sg_dram = idram("sg_dram", [32, 128, TL], BF16)
    x2_dram = idram("x2_dram", [8, 128, D])
    bcrows = idram("bcrows", [32, TL], BF16)
    modv = idram("modv", [1, 12288])
    sc_dram = idram("sc_dram", [1, D])
    rowbf_dram = idram("rowbf_dram", [1, D], BF16)

    st = contextlib.ExitStack()
    p = P(nc, st)

    def sball(stack):
        def f(name, shape, dt=F32):
            return stack.enter_context(nc.sbuf_tensor(name, list(shape), dt))
        return f

    sbp = sball(st)
    a_sl = sbp("a_sl", [128, 64])
    dtb_sl = sbp("dtb_sl", [128, 4])
    d_sl = sbp("d_sl", [128, 32])
    cw_sl = sbp("cw_sl", [128, 128])
    cb_sl = sbp("cb_sl", [128, 32])
    mask_sl = sbp("mask_sl", [128, 3], BF16)
    carry = sbp("carry", [128, 64])
    wdt_sl = sbp("wdt_sl", [128, 512], BF16)
    consts = sbp("consts", [128, 4])

    block = st.enter_context(nc.Block())

    with contextlib.ExitStack() as ph:
        psb = sball(ph)
        f0 = psb("f0", [128, 512])
        modrow = psb("modrow", [1, 12288])
        p.dma("sync", a_sl[:, :].rearrange("p (q s) -> p q s", q=4), Ain[:, :, :].transpose([1, 0, 2]))
        p.dma("sync", dtb_sl[:, :].rearrange("p (q o) -> p q o", q=4), dtb[:, :, :].transpose([1, 0, 2]))
        p.dma("sync", d_sl[:, :].rearrange("p (q o) -> p q o", q=32), Din[:, :, :].transpose([1, 0, 2]))
        p.dma("sync", cw_sl[:, :].rearrange("p (q k) -> p q k", q=32), cw[:, :, :].transpose([1, 0, 2]))
        p.dma("sync", cb_sl[:, :].rearrange("p (q o) -> p q o", q=32), cb[:, :, :].transpose([1, 0, 2]))
        p.dma("sync", f0[:, 0:3], mask3[:, :])
        p.add("vector", lambda e: e.tensor_copy(out=mask_sl[:], in_=f0[:, 0:3]))
        p.add("vector", lambda e: e.memset(carry[:], 0.0))
        p.add("vector", lambda e: e.memset(consts[:, 0:1], EPS))
        p.add("vector", lambda e: e.memset(consts[:, 1:2], 10.0))
        p.add("vector", lambda e: e.memset(consts[:, 2:3], 0.1))
        p.dma("sync", f0[:, 0:512].rearrange("p (q r) -> p q r", q=4), wdt[:, :, :].transpose([1, 0, 2]))
        p.add("vector", lambda e: e.tensor_copy(out=wdt_sl[:], in_=f0[:, 0:512]))

        r0 = psb("r0", [1, D])
        r1 = psb("r1", [1, D])
        r2 = psb("r2", [1, D])
        m0 = psb("m0", [1, D], U8)
        sccols = psb("sccols", [128, 16], BF16)
        p.dma("sync", r0[:, :], cvec[:, :])
        p.add("scalar", lambda e: e.activation(out=r1[:], in_=r0[:], func=AF.Silu))
        p.dma("gpsimd", sc_dram[:, :], r1[:, :])
        scc_f = psb("scc_f", [128, 16])
        p.dma("sync", scc_f[:, :], sc_dram[0, :].rearrange("(k p) -> p k", p=128))
        p.add("vector", lambda e: e.tensor_copy(out=sccols[:], in_=scc_f[:]))

        wm_f = psb("wm_f", [128, 1536])
        wm_b = psb("wm_b", [128, 1536], BF16)
        mod_ps = ph.enter_context(nc.psum_tensor("mod_ps", [1, 1536], F32))
        modslice = psb("modslice", [1, 1536])
        for k in range(16):
            p.dma("sync", wm_f[:, :], wmod_j[k, :, :])
            p.add("vector", lambda e: e.tensor_copy(out=wm_b[:], in_=wm_f[:]))
            for c in range(3):
                p.add("tensor", lambda e, k=k, c=c: e.matmul(
                    mod_ps[:, c * 512:(c + 1) * 512], sccols[:, k:k + 1],
                    wm_b[:, c * 512:(c + 1) * 512], start=(k == 0), stop=(k == 15)))
        p.add("vector", lambda e: e.tensor_copy(out=modslice[:], in_=mod_ps[:]))
        p.dma("gpsimd", agm_in[:, :], modslice[:, :])
        p.add("gpsimd", lambda e: e.collective_compute(
            "AllGather", ALU.bypass, replica_groups=[list(range(NC8))],
            ins=[drh["agm_in"].ap().opt()], outs=[drh["agm_out"].ap().opt()]))
        p.dma("sync", modrow[:, :], agm_out[:, :].rearrange("a b -> (a b)")[None, :])
        for b0 in (2 * D, 5 * D):
            p.add("vector", lambda e, b0=b0: e.tensor_single_scalar(
                out=m0[:], in_=modrow[:, b0:b0 + D], scalar=0.0, op=ALU.is_ge))
            p.add("vector", lambda e, b0=b0: e.tensor_scalar_max(
                out=r0[:], in0=modrow[:, b0:b0 + D], scalar1=0.25))
            p.add("vector", lambda e, b0=b0: e.tensor_scalar_min(
                out=r1[:], in0=modrow[:, b0:b0 + D], scalar1=-0.25))
            p.add("vector", lambda e, b0=b0: e.select(
                out=modrow[:, b0:b0 + D], mask=m0[:], on_true=r0[:], on_false=r1[:]))
        for b0 in (D, 4 * D):
            p.add("vector", lambda e, b0=b0: e.tensor_scalar_add(
                out=modrow[:, b0:b0 + D], in0=modrow[:, b0:b0 + D], scalar1=1.0))
        p.dma("gpsimd", modv[:, :], modrow[:, :])

    p.new_phase()
    with contextlib.ExitStack() as ph:
        psb = sball(ph)
        ca1b = psb("ca1b", [128, D])
        sab = psb("sab", [128, D])
        p.dma("sync", ca1b[:, :], modv[0, D:2 * D][None, :].broadcast_to([128, D]))
        p.dma("sync", sab[:, :], modv[0, 0:D][None, :].broadcast_to([128, D]))

        hT = [psb(f"hT{i}", [128, TH], BF16) for i in range(16)]
        xtile = psb("xtile", [128, D])
        sqt = psb("sqt", [128, D])
        stat = psb("stat", [128, 4])
        xhat = psb("xhat", [128, D])
        htm = psb("htm", [128, D], BF16)
        for tt in range(9):
            p.dma("sync", xtile[:, :], xs[tt * 128:(tt + 1) * 128, :])
            p.add("vector", lambda e: e.tensor_reduce(
                out=stat[:, 0:1], in_=xtile[:], axis=AXX, op=ALU.add))
            p.add("scalar", lambda e: e.square(out=sqt[:], in_=xtile[:]))
            p.add("vector", lambda e: e.tensor_reduce(
                out=stat[:, 1:2], in_=sqt[:], axis=AXX, op=ALU.add))
            p.add("vector", lambda e: e.tensor_scalar_mul(
                out=stat[:, 2:3], in0=stat[:, 0:1], scalar1=1.0 / D))
            p.add("scalar", lambda e: e.square(out=stat[:, 0:1], in_=stat[:, 2:3]))
            p.add("vector", lambda e: e.scalar_tensor_tensor(
                out=stat[:, 1:2], in0=stat[:, 1:2], scalar=1.0 / D,
                in1=stat[:, 0:1], op0=ALU.mult, op1=ALU.subtract))
            p.add("scalar", lambda e: e.activation(
                out=stat[:, 3:4], in_=stat[:, 1:2], func=AF.Sqrt, bias=consts[:, 0:1]))
            p.add("vector", lambda e: e.reciprocal(
                out=stat[:, 3:4], in_=stat[:, 3:4]))
            p.add("vector", lambda e: e.tensor_scalar(
                out=xhat[:], in0=xtile[:], scalar1=stat[:, 2:3], scalar2=stat[:, 3:4],
                op0=ALU.subtract, op1=ALU.mult))
            p.add("vector", lambda e: e.tensor_mul(out=xhat[:], in0=xhat[:], in1=ca1b[:]))
            p.add("vector", lambda e: e.tensor_add(out=htm[:], in0=xhat[:], in1=sab[:]))
            for di in range(16):
                p.dmaT("sync", hT[di][:, tt * 128:(tt + 1) * 128],
                       htm[:, di * 128:(di + 1) * 128])
        for di in range(16):
            p.add("vector", lambda e, di=di: e.tensor_mul(
                out=hT[di][:, HALO - 3:HALO], in0=hT[di][:, HALO - 3:HALO],
                in1=mask_sl[:]))

        wraw = psb("wraw", [128, 2048])
        wbf = psb("wbf", [128, 2048], BF16)
        ip0 = ph.enter_context(nc.psum_tensor("ip0", [128, 512], F32))
        ip1 = ph.enter_context(nc.psum_tensor("ip1", [128, 512], F32))
        ip2 = ph.enter_context(nc.psum_tensor("ip2", [128, 512], F32))
        ipp = [ip0, ip1, ip2]
        FDC = [(0, 512), (512, 1024), (1024, TH)]
        xh = psb("xh", [128, TH], BF16)
        xc0 = psb("xc0", [128, TL], BF16)
        xc1 = psb("xc1", [128, TL], BF16)
        xt_t = [psb(f"xt{i}", [128, TL], BF16) for i in range(32)]
        sgt = psb("sgt", [128, TL], BF16)

        for et in range(64):
            p.dma("sync", wraw[:, :], win_t[et, :, :])
            p.add("vector", lambda e: e.tensor_copy(out=wbf[:], in_=wraw[:]))
            for k in range(16):
                for ci, (c0, c1) in enumerate(FDC):
                    p.add("tensor", lambda e, k=k, ci=ci, c0=c0, c1=c1: e.matmul(
                        ipp[ci][:, 0:c1 - c0], wbf[:, k * 128:(k + 1) * 128],
                        hT[k][:, c0:c1], start=(k == 0), stop=(k == 15)))
            if et < 32:
                for ci, (c0, c1) in enumerate(FDC):
                    p.add("scalar", lambda e, ci=ci, c0=c0, c1=c1: e.activation(
                        out=xh[:, c0:c1], in_=ipp[ci][:, 0:c1 - c0], func=AF.Copy))
                p.add("vector", lambda e, et=et: e.tensor_scalar_mul(
                    out=xc0[:], in0=xh[:, HALO - 3:HALO - 3 + TL],
                    scalar1=cw_sl[:, 4 * et:4 * et + 1]))
                p.add("vector", lambda e, et=et: e.scalar_tensor_tensor(
                    out=xc1[:], in0=xh[:, HALO - 2:HALO - 2 + TL],
                    scalar=cw_sl[:, 4 * et + 1:4 * et + 2], in1=xc0[:],
                    op0=ALU.mult, op1=ALU.add))
                p.add("vector", lambda e, et=et: e.scalar_tensor_tensor(
                    out=xc0[:], in0=xh[:, HALO - 1:HALO - 1 + TL],
                    scalar=cw_sl[:, 4 * et + 2:4 * et + 3], in1=xc1[:],
                    op0=ALU.mult, op1=ALU.add))
                p.add("vector", lambda e, et=et: e.scalar_tensor_tensor(
                    out=xc1[:], in0=xh[:, HALO:HALO + TL],
                    scalar=cw_sl[:, 4 * et + 3:4 * et + 4], in1=xc0[:],
                    op0=ALU.mult, op1=ALU.add))
                p.add("scalar", lambda e, et=et: e.activation(
                    out=xt_t[et][:], in_=xc1[:], func=AF.Silu,
                    bias=cb_sl[:, et:et + 1]))
                p.dma("gpsimd", a2a_in[et // 4, et % 4, :, :], xt_t[et][:, :])
            else:
                p.add("scalar", lambda e: e.activation(
                    out=sgt[:, 0:384], in_=ip0[:, 128:512], func=AF.Silu))
                p.add("scalar", lambda e: e.activation(
                    out=sgt[:, 384:896], in_=ip1[:, 0:512], func=AF.Silu))
                p.add("scalar", lambda e: e.activation(
                    out=sgt[:, 896:1024], in_=ip2[:, 0:128], func=AF.Silu))
                p.dma("gpsimd", sg_dram[et - 32, :, :], sgt[:, :])

        wxs = psb("wxs", [128, 160])
        wxb = psb("wxb", [128, 160], BF16)
        xp0 = ph.enter_context(nc.psum_tensor("xp0", [128, TL], F32))
        xp1 = ph.enter_context(nc.psum_tensor("xp1", [32, TL], F32))
        ssm_hi = psb("ssm_hi", [128, TL])
        ssm_lo = psb("ssm_lo", [32, TL])
        for k in range(32):
            p.dma("sync", wxs[:, :], wx_t[k, :, :])
            p.add("vector", lambda e: e.tensor_copy(out=wxb[:], in_=wxs[:]))
            for c0 in (0, 512):
                p.add("tensor", lambda e, k=k, c0=c0: e.matmul(
                    xp0[:, c0:c0 + 512], wxb[:, 0:128], xt_t[k][:, c0:c0 + 512],
                    start=(k == 0), stop=(k == 31)))
                p.add("tensor", lambda e, k=k, c0=c0: e.matmul(
                    xp1[:, c0:c0 + 512], wxb[:, 128:160], xt_t[k][:, c0:c0 + 512],
                    start=(k == 0), stop=(k == 31)))
        p.add("vector", lambda e: e.tensor_copy(out=ssm_hi[:], in_=xp0[:]))
        p.add("vector", lambda e: e.tensor_copy(out=ssm_lo[:], in_=xp1[:]))
        p.dma("gpsimd", ag_in[0:128, :], ssm_hi[:, :])
        p.dma("gpsimd", ag_in[128:160, :], ssm_lo[:, :])

    p.new_phase()

    def _a2a(e):
        with allow_ici_all_to_all():
            return e.collective_compute(
                "AllToAll", ALU.bypass, replica_groups=[list(range(NC8))],
                ins=[drh["a2a_in"].ap().opt()], outs=[drh["a2a_out"].ap().opt()])
    p.add("gpsimd", _a2a)
    p.add("gpsimd", lambda e: e.collective_compute(
        "AllGather", ALU.bypass, replica_groups=[list(range(NC8))],
        ins=[drh["ag_in"].ap().opt()], outs=[drh["ag_out"].ap().opt()]))

    p.new_phase()
    with contextlib.ExitStack() as ph:
        psb = sball(ph)
        xt4 = [psb(f"sxt{q}", [128, TL], BF16) for q in range(4)]
        tsf = psb("tsf", [128, TL])
        tsb = psb("tsb", [128, TL], BF16)
        bcf = psb("bcf", [32, TL])
        bcb = psb("bcb", [32, TL], BF16)
        Bb = [psb(f"Bb{s}", [128, TL], BF16) for s in range(16)]
        Cb = [psb(f"Cb{s}", [128, TL], BF16) for s in range(16)]
        dtq = psb("dtq", [128, TL])
        dte = psb("dte", [128, TL])
        uq = psb("uq", [128, TL], BF16)
        dA = psb("dA", [128, TL])
        uB = psb("uB", [128, TL], BF16)
        Stt = [psb(f"St{s}", [128, TL], BF16) for s in range(16)]
        dt_ps = ph.enter_context(nc.psum_tensor("dt_ps", [128, TL], F32))

        for c in range(8):
            for q in range(4):
                p.dma("sync", xt4[q][:], a2a_out[c, q, :, :])
            p.dma("sync", tsf[:, :], ag_out[c, 0:128, :])
            p.add("vector", lambda e: e.tensor_copy(out=tsb[:], in_=tsf[:]))
            p.dma("sync", bcf[:, :], ag_out[c, 128:160, :])
            p.add("vector", lambda e: e.tensor_copy(out=bcb[:], in_=bcf[:]))
            p.dma("gpsimd", bcrows[:, :], bcb[:, :])
            for s in range(16):
                p.dma("sync", Bb[s][:], bcrows[s, :][None, :].broadcast_to([128, TL]))
                p.dma("sync", Cb[s][:], bcrows[16 + s, :][None, :].broadcast_to([128, TL]))
            for q in range(4):
                for c0 in (0, 512):
                    p.add("tensor", lambda e, q=q, c0=c0: e.matmul(
                        dt_ps[:, c0:c0 + 512], wdt_sl[:, q * 128:(q + 1) * 128],
                        tsb[:, c0:c0 + 512], start=True, stop=True))
                p.add("scalar", lambda e, q=q: e.activation(
                    out=dte[:], in_=dt_ps[:], func=AF.Exp, bias=dtb_sl[:, q:q + 1]))
                p.add("scalar", lambda e: e.activation(
                    out=dtq[:], in_=dte[:], func=AF.Ln, bias=1.0))
                p.add("vector", lambda e, q=q: e.tensor_mul(
                    out=uq[:], in0=dtq[:], in1=xt4[q][:]))
                for s in range(16):
                    p.add("scalar", lambda e, q=q, s=s: e.activation(
                        out=dA[:], in_=dtq[:], func=AF.Exp,
                        scale=a_sl[:, q * 16 + s:q * 16 + s + 1]))
                    p.add("vector", lambda e, s=s: e.tensor_mul(
                        out=uB[:], in0=uq[:], in1=Bb[s][:]))
                    p.add("vector", lambda e, q=q, s=s: e.tensor_tensor_scan(
                        out=Stt[s][:], data0=dA[:], data1=uB[:],
                        initial=carry[:, q * 16 + s:q * 16 + s + 1],
                        op0=ALU.mult, op1=ALU.add))
                    p.add("vector", lambda e, q=q, s=s: e.tensor_copy(
                        out=carry[:, q * 16 + s:q * 16 + s + 1],
                        in_=Stt[s][:, TL - 1:TL]))
                    p.add("vector", lambda e, s=s: e.tensor_mul(
                        out=Stt[s][:], in0=Stt[s][:], in1=Cb[s][:]))
                for stp in (8, 4, 2, 1):
                    for s in range(stp):
                        p.add("vector", lambda e, s=s, stp=stp: e.tensor_add(
                            out=Stt[s][:], in0=Stt[s][:], in1=Stt[s + stp][:]))
                p.dma("gpsimd", a2y_in[c, q, :, :], Stt[0][:, :])

    p.new_phase()

    def _a2y(e):
        with allow_ici_all_to_all():
            return e.collective_compute(
                "AllToAll", ALU.bypass, replica_groups=[list(range(NC8))],
                ins=[drh["a2y_in"].ap().opt()], outs=[drh["a2y_out"].ap().opt()])
    p.add("gpsimd", _a2y)

    p.new_phase()
    with contextlib.ExitStack() as ph5:
        psb5 = sball(ph5)
        h2T = [psb5(f"h2T{i}", [128, TL], BF16) for i in range(16)]

        with contextlib.ExitStack() as ph:
            psb = sball(ph)
            yf = [psb(f"yf{i}", [128, TL], BF16) for i in range(8)]
            ytmp = psb("ytmp", [128, TL], BF16)
            sgl = psb("sgl", [128, TL], BF16)
            wosb = [psb(f"wosb{i}", [128, 2048], BF16) for i in range(8)]
            x2acc = [psb(f"x2acc{i}", [128, D]) for i in range(8)]
            scratch = psb("scratch", [128, 2048])
            op_ps = ph.enter_context(nc.psum_tensor("op_ps", [128, D], F32))
            for kg in range(4):
                for kk in range(8):
                    k = kg * 8 + kk
                    p.dma("sync", ytmp[:], a2y_out[k // 4, k % 4, :, :])
                    p.dma("sync", yf[kk][:], a2a_in[k // 4, k % 4, :, :])
                    p.dma("sync", sgl[:], sg_dram[k, :, :])
                    p.add("vector", lambda e, kk=kk, k=k: e.scalar_tensor_tensor(
                        out=yf[kk][:], in0=yf[kk][:], scalar=d_sl[:, k:k + 1],
                        in1=ytmp[:], op0=ALU.mult, op1=ALU.add))
                    p.add("vector", lambda e, kk=kk: e.tensor_mul(
                        out=yf[kk][:], in0=yf[kk][:], in1=sgl[:]))
                    p.dma("sync", scratch[:, :], wout_t[k, :, :])
                    p.add("vector", lambda e, kk=kk: e.tensor_copy(
                        out=wosb[kk][:], in_=scratch[:]))
                for tt in range(8):
                    for kk in range(8):
                        for c in range(4):
                            p.add("tensor", lambda e, kk=kk, c=c, tt=tt: e.matmul(
                                op_ps[:, c * 512:(c + 1) * 512],
                                yf[kk][:, tt * 128:(tt + 1) * 128],
                                wosb[kk][:, c * 512:(c + 1) * 512],
                                start=(kk == 0), stop=(kk == 7)))
                    if kg == 0:
                        p.add("vector", lambda e, tt=tt: e.tensor_copy(
                            out=x2acc[tt][:], in_=op_ps[:]))
                    else:
                        p.add("vector", lambda e, tt=tt: e.tensor_add(
                            out=x2acc[tt][:], in0=x2acc[tt][:], in1=op_ps[:]))

            gab = psb("gab", [128, D], BF16)
            cb1b = psb("cb1b", [128, D], BF16)
            sbb = psb("sbb", [128, D], BF16)
            mrow = psb("mrow", [1, D], BF16)
            for nm, tile, off in (("ga", gab, 2 * D), ("cb1", cb1b, 4 * D),
                                  ("sb", sbb, 3 * D)):
                p.dma("sync", scratch[0:1, 0:D], modv[0, off:off + D][None, :])
                p.add("vector", lambda e: e.tensor_copy(
                    out=mrow[:], in_=scratch[0:1, 0:D]))
                p.dma("gpsimd", rowbf_dram[:, :], mrow[:, :])
                p.dma("sync", tile[:, :], rowbf_dram[0, :][None, :].broadcast_to([128, D]))
            sq2 = psb("sq2", [128, D])
            stat2 = psb("stat2", [128, 4])
            h2tm = psb("h2tm", [128, D], BF16)
            for tt in range(8):
                p.dma("sync", scratch[:, :], xs[HALO + tt * 128:HALO + (tt + 1) * 128, :])
                p.add("vector", lambda e, tt=tt: e.tensor_mul(
                    out=x2acc[tt][:], in0=x2acc[tt][:], in1=gab[:]))
                p.add("vector", lambda e, tt=tt: e.tensor_add(
                    out=x2acc[tt][:], in0=x2acc[tt][:], in1=scratch[:]))
                p.dma("gpsimd", x2_dram[tt, :, :], x2acc[tt][:, :])
                p.add("vector", lambda e, tt=tt: e.tensor_reduce(
                    out=stat2[:, 0:1], in_=x2acc[tt][:], axis=AXX, op=ALU.add))
                p.add("scalar", lambda e, tt=tt: e.square(out=sq2[:], in_=x2acc[tt][:]))
                p.add("vector", lambda e: e.tensor_reduce(
                    out=stat2[:, 1:2], in_=sq2[:], axis=AXX, op=ALU.add))
                p.add("vector", lambda e: e.tensor_scalar_mul(
                    out=stat2[:, 2:3], in0=stat2[:, 0:1], scalar1=1.0 / D))
                p.add("scalar", lambda e: e.square(out=stat2[:, 0:1], in_=stat2[:, 2:3]))
                p.add("vector", lambda e: e.scalar_tensor_tensor(
                    out=stat2[:, 1:2], in0=stat2[:, 1:2], scalar=1.0 / D,
                    in1=stat2[:, 0:1], op0=ALU.mult, op1=ALU.subtract))
                p.add("scalar", lambda e: e.activation(
                    out=stat2[:, 3:4], in_=stat2[:, 1:2], func=AF.Sqrt, bias=consts[:, 0:1]))
                p.add("vector", lambda e: e.reciprocal(
                    out=stat2[:, 3:4], in_=stat2[:, 3:4]))
                p.add("vector", lambda e, tt=tt: e.tensor_scalar(
                    out=x2acc[tt][:], in0=x2acc[tt][:], scalar1=stat2[:, 2:3],
                    scalar2=stat2[:, 3:4], op0=ALU.subtract, op1=ALU.mult))
                p.add("vector", lambda e, tt=tt: e.tensor_mul(
                    out=x2acc[tt][:], in0=x2acc[tt][:], in1=cb1b[:]))
                p.add("vector", lambda e, tt=tt: e.tensor_add(
                    out=h2tm[:], in0=x2acc[tt][:], in1=sbb[:]))
                for di in range(16):
                    p.dmaT("sync", h2T[di][:, tt * 128:(tt + 1) * 128],
                           h2tm[:, di * 128:(di + 1) * 128])

        with contextlib.ExitStack() as ph:
            psb = sball(ph)
            w1f = psb("w1f", [128, 2048])
            w1b = psb("w1b", [128, 2048], BF16)
            w2b = [psb(f"w2b{i}", [128, 2048], BF16) for i in range(8)]
            af = [psb(f"af{i}", [128, TL], BF16) for i in range(16)]
            q10 = psb("q10", [128, 512])
            qi = psb("qi", [128, 512], I32)
            qf = psb("qf", [128, 512])
            ffnacc = [psb(f"ffnacc{i}", [128, TL]) for i in range(16)]
            f_ps = ph.enter_context(nc.psum_tensor("f_ps", [128, TL], F32))
            e_ps = ph.enter_context(nc.psum_tensor("e_ps", [128, TL], F32))
            for fb in range(4):
                for fi in range(16):
                    f = fb * 16 + fi
                    p.dma("sync", w1f[:, :], w1_t[f, :, :])
                    p.add("vector", lambda e: e.tensor_copy(out=w1b[:], in_=w1f[:]))
                    for k in range(16):
                        for c0 in (0, 512):
                            p.add("tensor", lambda e, k=k, c0=c0: e.matmul(
                                f_ps[:, c0:c0 + 512], w1b[:, k * 128:(k + 1) * 128],
                                h2T[k][:, c0:c0 + 512], start=(k == 0), stop=(k == 15)))
                    for c0 in (0, 512):
                        p.add("scalar", lambda e, c0=c0: e.activation(
                            out=q10[:], in_=f_ps[:, c0:c0 + 512], func=AF.Copy,
                            scale=consts[:, 1:2]))
                        p.add("vector", lambda e: e.tensor_copy(out=qi[:], in_=q10[:]))
                        p.add("vector", lambda e: e.tensor_copy(out=qf[:], in_=qi[:]))
                        p.add("vector", lambda e: e.tensor_scalar(
                            out=qf[:], in0=qf[:], scalar1=32767.0, scalar2=-32768.0,
                            op0=ALU.min, op1=ALU.max))
                        p.add("scalar", lambda e, fi=fi, c0=c0: e.activation(
                            out=af[fi][:, c0:c0 + 512], in_=qf[:],
                            func=AF.Gelu_apprx_tanh, scale=consts[:, 2:3]))
                for fsub in range(2):
                    for kk in range(8):
                        k = fb * 16 + fsub * 8 + kk
                        p.dma("sync", w1f[:, :], w2_t[k, :, :])
                        p.add("vector", lambda e, kk=kk: e.tensor_copy(
                            out=w2b[kk][:], in_=w1f[:]))
                    for et in range(16):
                        for kk in range(8):
                            fi = fsub * 8 + kk
                            for c0 in (0, 512):
                                p.add("tensor", lambda e, kk=kk, fi=fi, et=et, c0=c0:
                                      e.matmul(
                                          e_ps[:, c0:c0 + 512],
                                          w2b[kk][:, et * 128:(et + 1) * 128],
                                          af[fi][:, c0:c0 + 512],
                                          start=(kk == 0), stop=(kk == 7)))
                        if fb == 0 and fsub == 0:
                            p.add("vector", lambda e, et=et: e.tensor_copy(
                                out=ffnacc[et][:], in_=e_ps[:]))
                        else:
                            p.add("vector", lambda e, et=et: e.tensor_add(
                                out=ffnacc[et][:], in0=ffnacc[et][:], in1=e_ps[:]))

            gbb = psb("gbb", [128, D], BF16)
            mrow2 = psb("mrow2", [1, D], BF16)
            p.dma("sync", w1f[0:1, 0:D], modv[0, 5 * D:6 * D][None, :])
            p.add("vector", lambda e: e.tensor_copy(out=mrow2[:], in_=w1f[0:1, 0:D]))
            p.dma("gpsimd", rowbf_dram[:, :], mrow2[:, :])
            p.dma("sync", gbb[:, :], rowbf_dram[0, :][None, :].broadcast_to([128, D]))
            for et in range(16):
                p.add("vector", lambda e, et=et: e.tensor_copy(
                    out=af[et][:], in_=ffnacc[et][:]))
            ffnT = psb("ffnT", [128, D], BF16)
            otile = psb("otile", [128, D])
            x2r = w1f
            for tt in range(8):
                for et in range(16):
                    p.dmaT("sync", ffnT[:, et * 128:(et + 1) * 128],
                           af[et][:, tt * 128:(tt + 1) * 128])
                p.dma("sync", x2r[:, :], x2_dram[tt, :, :])
                p.add("vector", lambda e: e.tensor_mul(
                    out=otile[:], in0=ffnT[:], in1=gbb[:]))
                p.add("vector", lambda e: e.tensor_add(
                    out=otile[:], in0=otile[:], in1=x2r[:]))
                p.dma("gpsimd", out[tt * 128:(tt + 1) * 128, :], otile[:, :])

    with nc.allow_non_contiguous_dma(reason="small strided parameter loads"):
        p.emit(block)
    st.close()
    return nc


_CACHE = {}


def kernel(L0_attn_hidden_states, L0_attn_in_proj_w, L0_attn_conv_w, L0_attn_conv_b,
           L0_attn_x_proj_w, L0_attn_dt_proj_w, L0_attn_dt_proj_b, L0_attn_A,
           L0_attn_D, L0_attn_out_proj_w, L0_ffn_W1, L0_ffn_W2, L0_c, L0_wmod):
    x = np.ascontiguousarray(np.asarray(L0_attn_hidden_states, dtype=np.float32)[0])
    Win = np.asarray(L0_attn_in_proj_w, dtype=np.float32)
    cwf = np.asarray(L0_attn_conv_w, dtype=np.float32)
    cbf = np.asarray(L0_attn_conv_b, dtype=np.float32)
    Wx = np.asarray(L0_attn_x_proj_w, dtype=np.float32)
    Wdt = np.asarray(L0_attn_dt_proj_w, dtype=np.float32)
    dtbf = np.asarray(L0_attn_dt_proj_b, dtype=np.float32)
    Af = np.asarray(L0_attn_A, dtype=np.float32)
    Df = np.asarray(L0_attn_D, dtype=np.float32)
    Wout = np.asarray(L0_attn_out_proj_w, dtype=np.float32)
    W1 = np.asarray(L0_ffn_W1, dtype=np.float32)
    W2 = np.asarray(L0_ffn_W2, dtype=np.float32)
    cv = np.asarray(L0_c, dtype=np.float32).reshape(1, D)
    wmod = np.asarray(L0_wmod, dtype=np.float32)

    win_t = np.ascontiguousarray(
        Win.reshape(64, 128, 16, 128).transpose(0, 3, 2, 1).reshape(64, 128, 2048))
    wx_t = np.ascontiguousarray(Wx.T.reshape(32, 128, 160))
    wout_t = np.ascontiguousarray(Wout.T.reshape(32, 128, 2048))
    w1_t = np.ascontiguousarray(
        W1.reshape(64, 128, 16, 128).transpose(0, 3, 2, 1).reshape(64, 128, 2048))
    w2_t = np.ascontiguousarray(
        W2.reshape(16, 128, 64, 128).transpose(2, 3, 0, 1).reshape(64, 128, 2048))
    cw_t = np.ascontiguousarray(cwf.reshape(32, 128, 4))
    cb_t = cbf.reshape(32, 128, 1)
    d_t = Df.reshape(32, 128, 1)

    if "nc" not in _CACHE:
        _CACHE["nc"] = build_program()
    nc = _CACHE["nc"]

    in_maps = []
    for j in range(NC8):
        t0 = j * TL
        if j == 0:
            xsj = np.concatenate([np.zeros((HALO, D), np.float32), x[0:TL]], axis=0)
        else:
            xsj = x[t0 - HALO:t0 + TL]
        dsl = slice(j * 512, (j + 1) * 512)
        wdt_j = np.ascontiguousarray(Wdt[dsl].reshape(4, 128, 128).transpose(0, 2, 1))
        in_maps.append({
            "xs": np.ascontiguousarray(xsj),
            "win_t": win_t,
            "cw": cw_t, "cb": cb_t,
            "wx_t": wx_t,
            "wdt": wdt_j,
            "dtb": dtbf[dsl].reshape(4, 128, 1),
            "Ain": np.ascontiguousarray(Af[dsl].reshape(4, 128, 16)),
            "Din": d_t,
            "wout_t": wout_t,
            "w1_t": w1_t, "w2_t": w2_t,
            "cvec": cv,
            "wmod_j": np.ascontiguousarray(
                wmod[:, j * 1536:(j + 1) * 1536].reshape(16, 128, 1536)),
            "mask3": (np.zeros((128, 3), np.float32) if j == 0
                      else np.ones((128, 3), np.float32)),
        })

    res = run_bass_kernel_spmd(nc, in_maps, core_ids=list(range(NC8)))
    outs = [res.results[j]["out"] for j in range(NC8)]
    return np.concatenate(outs, axis=0).reshape(1, T, D)
